# version 41
# baseline (speedup 1.0000x reference)
import numpy as np
import concourse.bass as bass
import concourse.mybir as mybir
from concourse.bass_utils import run_bass_kernel_spmd
from contextlib import ExitStack

F32 = mybir.dt.float32
BF16 = mybir.dt.bfloat16
U8 = mybir.dt.uint8
AF = mybir.ActivationFunctionType
ALU = mybir.AluOpType
AX = mybir.AxisListType

T, D = 2048, 1024
NT, NCH = 16, 8
NCK = 32
NEG = -30000.0
EPS = 1e-5

ENGS = ("sync", "tensor", "vector", "scalar", "gpsimd")


class Sch:

    def __init__(self, sem_names):
        self.ops = {e: [] for e in ENGS}
        self.cnt = {s: 0 for s in sem_names}

    def op(self, eng, fn, waits=(), inc=None, n=1):
        for s, v in waits:
            assert v <= self.cnt[s], (s, v, self.cnt[s])
        self.ops[eng].append((tuple((s, v) for s, v in waits if v > 0), fn, inc, n))
        if inc is not None:
            self.cnt[inc] += n
            return self.cnt[inc]
        return None


def build_nc():
    nc = bass.Bass("TRN2", target_bir_lowering=False)
    x_h = nc.dram_tensor("x", [T, D], F32, kind="ExternalInput").ap()
    wkT_h = nc.dram_tensor("wkT", [D, D], F32, kind="ExternalInput").ap()
    wvT_h = nc.dram_tensor("wvT", [D, D], F32, kind="ExternalInput").ap()
    abp_h = nc.dram_tensor("abp", [2560, D], F32, kind="ExternalInput").ap()
    wc_h = nc.dram_tensor("wcT", [16384, D], F32, kind="ExternalInput").ap()
    y_h = nc.dram_tensor("y", [T, D], F32, kind="ExternalOutput").ap()

    es = ExitStack()
    ARENA = 212800
    arena = es.enter_context(nc.sbuf_tensor("arena", [128, ARENA], U8))
    base = nc.lookup_mloc(arena).addr

    def sb(name, shape, dtype, off):
        return nc.alloc_sbuf_tensor_at(name, shape, dtype, offset=base + off)

    x_t = sb("x_t", [128, NT, 1024], F32, 0)
    eye = sb("eye", [128, 128], BF16, 65536)
    colc = sb("colc", [128, 128], F32, 66048)
    rowi = sb("rowi", [128, 1], F32, 66560)
    bnst = sb("bnst", [128, 2, 12], F32, 66592)
    mv = sb("mv", [128, 16, 2], F32, 66720)
    tmpln = sb("tmpln", [128, 16], F32, 66848)
    rstd = sb("rstd", [128, 16], F32, 66912)
    negmean = sb("negmean", [128, 16], F32, 66976)
    nmr = sb("nmr", [128, 16], F32, 67040)
    t1s = sb("t1s", [128, 16], F32, 67104)
    negmax = sb("negmax", [128, 16], F32, 67168)
    rsum = sb("rsum", [128, 16], F32, 67232)
    rl = sb("rl", [128, 16], F32, 67296)
    ceps = sb("ceps", [128, 1], F32, 67360)
    chalf = sb("chalf", [128, 1], F32, 67392)
    cneg1 = sb("cneg1", [128, 1], F32, 67424)
    cone = sb("cone", [128, 1], F32, 67456)
    pmax = sb("pmax", [128, 2, 4], F32, 67488)
    PT_s = sb("PT_s", [128, 2, 128], BF16, 67520)
    wk_bf = sb("wk_bf", [128, 8, 1024], BF16, 68096)
    wv_bf = sb("wv_bf", [128, 8, 1024], BF16, 84480)
    P_sb = [sb("P_sa", [128, 2048], BF16, 68096),
            sb("P_sc", [128, 2048], BF16, 72192),
            sb("P_se", [128, 2048], BF16, 76288)]
    wbuf = sb("wbuf", [128, 8, 1024], BF16, 68096)
    WSL = ((6, 7, 2, 3), (4, 5, 0, 1))
    ab_b2 = sb("ab_b2", [128, 2, 1024], BF16, 207360)
    ab_b2b = sb("ab_b2b", [128, 2, 1024], BF16, 92672)
    abA = sb("abA", [128, 4, 1024], BF16, 84480)
    xlroll = sb("xlroll", [128, 4, 1024], BF16, 100864)
    xl_fm = sb("xl_fm", [128, NCH, 2048], BF16, 109056)
    kf = sb("kf", [128, NCH, 2048], BF16, 141824)
    railw = kf
    vt = sb("vt", [128, NT, 1024], BF16, 174592)
    h_sl = [sb("h_a", [128, 4, 2048], BF16, 174592),
            sb("h_b", [128, 4, 2048], BF16, 190976)]

    PS = nc.alloc_psum_tensor("PS", [128, 4096], F32).ap()
    PSb = PS.bitcast(BF16)

    sem_names = (["dX", "dX2", "dT0", "dT1", "dT2", "dT3", "dWK",
                  "dAB0", "dB0", "dB1", "dB2", "dB3", "dWB0", "dWB1", "dY",
                  "cP", "cV", "cA", "cG"] + ["dV%d" % c for c in range(16)])
    sems = {s: es.enter_context(nc.semaphore(s)) for s in sem_names}
    sch = Sch(sem_names)
    m = {}

    def W(*pairs):
        return [p for p in pairs if p is not None and p[1] is not None]

    nc.const_aps.register(F32, EPS, ceps.ap())
    nc.const_aps.register(F32, -0.5, chalf.ap())
    nc.const_aps.register(F32, -1.0, cneg1.ap())
    nc.const_aps.register(F32, 1.0, cone.ap())
    sch.op("gpsimd", lambda e: e.memset(ceps[:], EPS))
    sch.op("gpsimd", lambda e: e.memset(chalf[:], -0.5))
    sch.op("gpsimd", lambda e: e.memset(cneg1[:], -1.0))
    m["consts"] = sch.op("gpsimd", lambda e: e.memset(cone[:], 1.0), inc="cG")
    sch.op("gpsimd", lambda e: e.iota(colc[:], pattern=[[1, 128]], channel_multiplier=0,
                                      allow_small_or_imprecise_dtypes=True))
    m["iota"] = sch.op("gpsimd", lambda e: e.iota(rowi[:], pattern=[[0, 1]], channel_multiplier=1,
                                                  allow_small_or_imprecise_dtypes=True), inc="cG")
    m["eye"] = sch.op("vector", lambda e: e.tensor_scalar(out=eye[:], in0=colc[:], scalar1=rowi[:],
                                                          scalar2=None, op0=ALU.is_equal),
                      waits=[("cG", m["iota"])], inc="cV")
    m["mz"] = sch.op("gpsimd", lambda e: e.memset(colc[:], 0.0), waits=[("cV", m["eye"])], inc="cG")
    m["mask"] = sch.op("gpsimd", lambda e: e.affine_select(out=colc[:], in_=colc[:],
                                                           pattern=[[-1, 128]], compare_op=ALU.is_ge,
                                                           fill=NEG, channel_multiplier=1),
                       waits=[("cG", m["mz"])], inc="cG")
    causal = colc

    for q in range(NT):
        eng = "sync" if q % 2 == 0 else "scalar"
        sem = f"dT{q}" if q < 4 else ("dX" if q % 2 == 0 else "dX2")
        def f(e, q=q):
            return e.dma_start(out=x_t[:, q, :], in_=x_h[q * 128:(q + 1) * 128, :])
        sch.op(eng, f, inc=sem, n=16)
    for q in range(NT):
        sem = f"dT{q}" if q < 4 else ("dX" if q % 2 == 0 else "dX2")
        m["dx", q] = (sem, sch.cnt[sem])

    for i in range(16):
        c2, h = i // 2, i % 2
        def fdma(e, c2=c2, h=h):
            return e.dma_start(out=wv_bf[:, c2, h * 512:(h + 1) * 512],
                               in_=wvT_h[c2 * 128:(c2 + 1) * 128, h * 512:(h + 1) * 512])
        sch.op("gpsimd", fdma, inc="dV%d" % i, n=16)
    for c2 in range(8):
        def fkdma(e, c2=c2):
            return e.dma_start(out=wk_bf[:, c2, :], in_=wkT_h[c2 * 128:(c2 + 1) * 128, :])
        sch.op("gpsimd", fkdma, inc="dWK", n=16)
    m["wk_all"] = ("dWK", sch.cnt["dWK"])

    def emit_v(qt):
        po = 2048 + (qt % 2) * 1024
        for c in range(NCH):
            for eh in range(2):
                def fv(e, qt=qt, c=c, eh=eh, po=po):
                    return e.matmul(PS[:, po + eh * 512: po + eh * 512 + 512],
                                    xl_fm[:, c, qt * 128:(qt + 1) * 128],
                                    wv_bf[:, c, eh * 512:(eh + 1) * 512],
                                    start=(c == 0), stop=(c == NCH - 1))
                w = []
                if c == 0 and eh == 0:
                    w = W(("cA", m["lnTe", qt]),
                          ("cV", m.get(("ve", qt - 2))) if qt >= 2 else None)
                if qt <= 1:
                    w.append(("dV%d" % (2 * c + eh), 16))
                last = (c == NCH - 1 and eh == 1)
                r = sch.op("tensor", fv, waits=w, inc="cP" if last else None)
                if last:
                    m["vmm", qt] = r

    def emit_ve(qt):
        po = 2048 + (qt % 2) * 1024
        def fve(e, qt=qt, po=po):
            return e.tensor_copy(out=vt[:, qt, :], in_=PS[:, po:po + 1024])
        m["ve", qt] = sch.op("vector", fve, waits=[("cP", m["vmm", qt])], inc="cV")

    for q in range(NT):
        def f1(e, q=q):
            return e.bn_stats(out=bnst[:, q % 2, 0:6], in_=x_t[:, q, 0:512])
        def f2(e, q=q):
            return e.bn_stats(out=bnst[:, q % 2, 6:12], in_=x_t[:, q, 512:1024])
        def f3(e, q=q):
            return e.bn_aggr(out=mv[:, q, :], in_=bnst[:, q % 2, :])
        sch.op("vector", f1, waits=[m["dx", q]], inc="cV")
        m["bns", q] = sch.op("vector", f2, inc="cV")
        m["aggr", q] = sch.op("vector", f3, waits=[("cV", m["bns", q])], inc="cV")
        def f4(e, q=q):
            return e.activation(out=tmpln[:, q:q + 1], in_=mv[:, q, 1:2], func=AF.Ln, bias=EPS)
        m["ln", q] = sch.op("scalar", f4, waits=W(("cV", m["aggr", q]),
                                                  ("cG", m["consts"]) if q == 0 else None), inc="cA")
        def f5(e, q=q):
            return e.activation(out=rstd[:, q:q + 1], in_=tmpln[:, q:q + 1], func=AF.Exp, scale=-0.5)
        m["rstd", q] = sch.op("scalar", f5, waits=[("cA", m["ln", q])], inc="cA")
        def f6(e, q=q):
            return e.activation(out=negmean[:, q:q + 1], in_=mv[:, q, 0:1], func=AF.Identity,
                                scale=-1.0)
        m["ngm", q] = sch.op("scalar", f6, inc="cA")
        def f7(e, q=q):
            return e.activation(out=nmr[:, q:q + 1], in_=negmean[:, q:q + 1], func=AF.Identity,
                                scale=rstd[:, q:q + 1])
        m["nmr1", q] = sch.op("scalar", f7, waits=[("cA", m["ngm", q])], inc="cA")
        def f8(e, q=q):
            return e.activation(out=t1s[:, q:q + 1], in_=rstd[:, q:q + 1], func=AF.Identity,
                                bias=1.0)
        sch.op("scalar", f8, waits=[("cA", m["rstd", q])])
        def f9(e, q=q):
            return e.activation(out=xlroll[:, q % 2, :], in_=x_t[:, q, :], func=AF.Identity,
                                scale=rstd[:, q:q + 1], bias=nmr[:, q:q + 1])
        m["xl", q] = sch.op("scalar", f9, waits=[("cA", m["nmr1", q])], inc="cA")
        def f10(e, q=q):
            return e.tensor_scalar(out=x_t[:, q, :], in0=x_t[:, q, :], scalar1=t1s[:, q:q + 1],
                                   scalar2=nmr[:, q:q + 1], op0=ALU.mult, op1=ALU.add)
        m["t1", q] = sch.op("gpsimd", f10, waits=[("cA", m["xl", q])], inc="cG")
        po = (q % 2) * 1024
        for c in range(NCH):
            def ft(e, q=q, c=c, po=po):
                return e.transpose(PSb[:, po + c * 128: po + c * 128 + 128],
                                   xlroll[:, q % 2, c * 128:c * 128 + 128], eye[:])
            w = W(("cA", m["xl", q]) if c == 0 else None,
                  ("cV", m["eye"]) if q == 0 and c == 0 else None,
                  ("cA", m.get(("lnTe", q - 2))) if q >= 2 and c == 0 else None)
            r = sch.op("tensor", ft, waits=w, inc="cP" if c == NCH - 1 else None)
            if c == NCH - 1:
                m["lnT", q] = r
        def fe(e, q=q, po=po):
            src = PSb[:, po:po + 1024].rearrange("p (c k) -> p c k", k=128)
            return e.copy(out=xl_fm[:, :, q * 128:q * 128 + 128], in_=src)
        m["lnTe", q] = sch.op("scalar", fe, waits=[("cP", m["lnT", q])], inc="cA")
        if q >= 1:
            emit_v(q - 1)
        if q >= 2:
            emit_ve(q - 2)
    emit_v(NT - 1)
    emit_ve(NT - 2)
    emit_ve(NT - 1)
    m["ve_all"] = m["ve", NT - 1]

    ii = 0
    for j in range(NCH):
        for th in range(2):
            po = ((2 * j + th) % 2) * 1024
            for c in range(NCH):
                for kbb in range(2):
                    def fk(e, j=j, th=th, c=c, kbb=kbb, po=po):
                        return e.matmul(PS[:, po + kbb * 512: po + kbb * 512 + 512],
                                        wk_bf[:, c, j * 128:(j + 1) * 128],
                                        xl_fm[:, c, th * 1024 + kbb * 512: th * 1024 + kbb * 512 + 512],
                                        start=(c == 0), stop=(c == NCH - 1))
                    w = []
                    if ii == 0:
                        w = [("cA", m["lnTe", NT - 1]), m["wk_all"]]
                    elif c == 0 and kbb == 0 and ii >= 2:
                        w = W(("cA", m.get(("ke", ii - 2))))
                    last = (c == NCH - 1 and kbb == 1)
                    r = sch.op("tensor", fk, waits=w, inc="cP" if last else None)
                    if last:
                        m["kmm", ii] = r
            def fke(e, j=j, th=th, po=po):
                return e.copy(out=kf[:, j, th * 1024:(th + 1) * 1024], in_=PS[:, po:po + 1024])
            m["ke", ii] = sch.op("scalar", fke, waits=[("cP", m["kmm", ii])], inc="cA")
            ii += 1
    m["ke_all"] = m["ke", ii - 1]

    for t in range(4):
        def ha(e, t=t):
            return e.dma_start(out=abA[:, t, :], in_=abp_h[128 * t:128 * t + 128, :])
        w = [("cP", m["vmm", NT - 1])] if t == 0 else []
        sch.op("gpsimd", ha, waits=w, inc="dAB0", n=16)
    m["aball"] = sch.cnt["dAB0"]
    for i in range(2):
        def wd0(e, zc=i, slot=WSL[0][i]):
            return e.dma_start(out=wbuf[:, slot, :], in_=wc_h[zc * 128:(zc + 1) * 128, :])
        w = [("cP", m["kmm", 15])] if i == 0 else []
        sch.op("gpsimd", wd0, waits=w, inc="dWB0", n=16)

    def bsl(j):
        return ab_b2[:, j % 4, :] if j % 4 < 2 else ab_b2b[:, j % 4 - 2, :]

    def emit_md(i):
        if i % 2 == 0:
            j = i // 2
            def hb(e, j=j):
                return e.dma_start(out=bsl(j), in_=abp_h[512 + 128 * j:512 + 128 * j + 128, :])
            w = W(("cP", m.get(("md", 2 * j - 7))))
            bsem = "dB%d" % (j % 4)
            sch.op("gpsimd", hb, waits=w, inc=bsem, n=16)
            m["bload", j] = (bsem, sch.cnt[bsem])
        hh, c = i // 8, i % 8
        po = (i % 2) * 512
        def hm(e, po=po, i=i):
            return e.matmul(PS[:, po:po + 512], abA[:, i // 8, (i % 8) * 128:(i % 8) * 128 + 128],
                            bsl(i // 2)[:, (i % 2) * 512:(i % 2) * 512 + 512],
                            start=True, stop=True)
        w = [m["bload", i // 2]]
        if i == 0:
            w += [("dAB0", m["aball"]), ("cA", m["exp", NT - 1])]
        if i >= 2:
            w.append(m["mde", i - 2])
        m["md", i] = sch.op("tensor", hm, waits=w, inc="cP")
        if i % 2 == 0:
            def he(e, hh=hh, c=c, po=po):
                return e.copy(out=railw[:, c, hh * 512:(hh + 1) * 512], in_=PS[:, po:po + 512])
            eng, sem = "scalar", "cA"
        else:
            def he(e, hh=hh, c=c, po=po):
                return e.tensor_copy(out=railw[:, c, hh * 512:(hh + 1) * 512],
                                     in_=PS[:, po:po + 512])
            eng, sem = "vector", "cV"
        w = [("cP", m["md", i])]
        if i == 0:
            w.append(("cP", m["smm", NT - 1]))
        m["mde", i] = (sem, sch.op(eng, he, waits=w, inc=sem))

    def emit_xl2(q):
        def g7(e, q=q):
            return e.tensor_scalar(out=xlroll[:, q % 4, :], in0=x_t[:, q, :],
                                   scalar1=rstd[:, q:q + 1], scalar2=nmr[:, q:q + 1],
                                   op0=ALU.mult, op1=ALU.add)
        w = [("cA", m["nm2", q])]
        if q >= 4:
            w.append(("cP", m["lnT2", q - 4]))
        m["xl2", q] = ("cG", sch.op("gpsimd", g7, waits=w, inc="cG"))

    def emit_pt(q, kt):
        po = 6144 + (kt % 2) * 1024
        def ft2(e, q=q, kt=kt, po=po):
            return e.transpose(PSb[:, po:po + 128], P_sb[q % 3][:, kt * 128:(kt + 1) * 128],
                               eye[:])
        w = W(("cA", m["exp", q]) if kt <= 1 else None,
              ("cV", m["ve_all"]) if q == 0 and kt <= 1 else None,
              m.get(("pte", q, kt - 2)) if kt >= 2 else None,
              ("cV", m.get(("x2", q - 1))) if q >= 1 and kt <= 1 else None)
        m["pt", q, kt] = sch.op("tensor", ft2, waits=w, inc="cP")

    def emit_pte(q, kt, eng):
        po = 6144 + (kt % 2) * 1024
        if eng == "scalar":
            def fpe(e, q=q, kt=kt, po=po):
                return e.copy(out=PT_s[:, kt % 2, :], in_=PSb[:, po:po + 128])
        else:
            def fpe(e, q=q, kt=kt, po=po):
                return e.tensor_copy(out=PT_s[:, kt % 2, :], in_=PSb[:, po:po + 128])
        sem = {"scalar": "cA", "vector": "cV"}[eng]
        m["pte", q, kt] = (sem, sch.op(eng, fpe, waits=[("cP", m["pt", q, kt])], inc=sem))

    def emit_pv(q, kt):
        for eh in range(2):
            def fpv(e, q=q, kt=kt, eh=eh):
                return e.matmul(PS[:, 2048 + eh * 512: 2048 + eh * 512 + 512],
                                PT_s[:, kt % 2, :], vt[:, kt, eh * 512:(eh + 1) * 512],
                                start=(kt == 0), stop=(kt == q))
            w = []
            if eh == 0:
                w = W(m["pte", q, kt],
                      ("cV", m.get(("x2", q - 1))) if q >= 1 and kt == 0 else None)
            r = sch.op("tensor", fpv, waits=w, inc="cP" if eh == 1 else None)
            if eh == 1:
                m["pv", q, kt] = r

    def emit_ptpv_rest(q):
        emit_pv(q, 0)
        for kt in range(2, q + 1):
            emit_pt(q, kt)
            emit_pte(q, kt, "scalar" if kt >= 6 else "vector")
            emit_pv(q, kt - 1)
            if q == NT - 1 and kt in (6, 10):
                i0 = 0 if kt == 6 else 2
                emit_md(i0)
                emit_md(i0 + 1)
        if q >= 1:
            emit_pv(q, q)
        def fr(e, q=q):
            return e.reciprocal(out=rl[:, q:q + 1], in_=rsum[:, q:q + 1])
        m["rl", q] = sch.op("vector", fr, waits=[("cA", m["exp", q])], inc="cV")
        def f11(e, q=q):
            return e.scalar_tensor_tensor(out=x_t[:, q, :], in0=PS[:, 2048:3072],
                                          scalar=rl[:, q:q + 1], in1=x_t[:, q, :],
                                          op0=ALU.mult, op1=ALU.add)
        m["x2", q] = sch.op("vector", f11,
                            waits=[("cP", m["pv", q, q]), ("cG", m["t1", q]),
                                   ("cV", m["rl", q])], inc="cV")
        def g1(e, q=q):
            return e.bn_stats(out=bnst[:, q % 2, 0:6], in_=x_t[:, q, 0:512])
        def g2(e, q=q):
            return e.bn_stats(out=bnst[:, q % 2, 6:12], in_=x_t[:, q, 512:1024])
        def g3(e, q=q):
            return e.bn_aggr(out=mv[:, q, :], in_=bnst[:, q % 2, :])
        sch.op("vector", g1, waits=[("cV", m["x2", q])], inc="cV")
        m["bns2", q] = sch.op("vector", g2, inc="cV")
        m["aggr2", q] = sch.op("vector", g3, waits=[("cV", m["bns2", q])], inc="cV")
        def g4(e, q=q):
            return e.activation(out=tmpln[:, q:q + 1], in_=mv[:, q, 1:2], func=AF.Ln, bias=EPS)
        m["ln2", q] = sch.op("scalar", g4, waits=[("cV", m["aggr2", q])], inc="cA")
        def g5(e, q=q):
            return e.activation(out=rstd[:, q:q + 1], in_=tmpln[:, q:q + 1], func=AF.Exp, scale=-0.5)
        m["rstd2", q] = sch.op("scalar", g5, waits=[("cA", m["ln2", q])], inc="cA")
        def g6(e, q=q):
            return e.activation(out=negmean[:, q:q + 1], in_=mv[:, q, 0:1], func=AF.Identity,
                                scale=-1.0)
        m["ngm2", q] = sch.op("scalar", g6, inc="cA")
        def g6b(e, q=q):
            return e.activation(out=nmr[:, q:q + 1], in_=negmean[:, q:q + 1], func=AF.Identity,
                                scale=rstd[:, q:q + 1])
        m["nm2", q] = sch.op("scalar", g6b, waits=[("cA", m["ngm2", q])], inc="cA")

    for q in range(NT + 2):
        if q < NT:
            Lk = (q + 1) * 128
            kbm = (Lk + 511) // 512
            sofs = (q % 2) * 1024 if q <= 7 else 0
            if q <= 7:
                for c in range(NCH):
                    for kb in range(kbm):
                        wdt = min(512, Lk - kb * 512)
                        def fs(e, q=q, c=c, kb=kb, wdt=wdt, sofs=sofs):
                            return e.matmul(PS[:, sofs + kb * 512: sofs + kb * 512 + wdt],
                                            xl_fm[:, c, q * 128:(q + 1) * 128],
                                            kf[:, c, kb * 512: kb * 512 + wdt],
                                            start=(c == 0), stop=(c == NCH - 1))
                        w = []
                        if c == 0 and kb == 0:
                            if q == 0:
                                w = [("cA", m["ke", 14])]
                            elif q == 1:
                                w = [("cA", m["ke_all"])]
                            else:
                                w = [("cA", m["exp", q - 2])]
                        last = (c == NCH - 1 and kb == kbm - 1)
                        r = sch.op("tensor", fs, waits=w, inc="cP" if last else None)
                        if last:
                            m["smm", q] = r
                def fm(e, q=q, sofs=sofs):
                    return e.tensor_tensor(out=PS[:, sofs + q * 128: sofs + (q + 1) * 128],
                                           in0=PS[:, sofs + q * 128: sofs + (q + 1) * 128],
                                           in1=causal[:], op=ALU.add)
                m["msk", q] = sch.op("vector", fm, waits=[("cP", m["smm", q]),
                                                          ("cG", m["mask"])], inc="cV")
                def fx(e, q=q, Lk=Lk, sofs=sofs):
                    return e.reduce_max(out=negmax[:, q:q + 1], in_=PS[:, sofs:sofs + Lk],
                                        axis=AX.X, negate=True)
                m["max", q] = sch.op("vector", fx, waits=[("cV", m["msk", q])], inc="cV")
            else:
                kd = (q * 128) // 512
                for kb in range(kbm):
                    wdt = min(512, Lk - kb * 512)
                    for c in range(NCH):
                        def fs(e, q=q, c=c, kb=kb, wdt=wdt):
                            return e.matmul(PS[:, kb * 512: kb * 512 + wdt],
                                            xl_fm[:, c, q * 128:(q + 1) * 128],
                                            kf[:, c, kb * 512: kb * 512 + wdt],
                                            start=(c == 0), stop=(c == NCH - 1))
                        w = []
                        if c == 0 and kb == 0:
                            w = [("cA", m["exp", 7 if q == 8 else q - 1])]
                        last = (c == NCH - 1)
                        r = sch.op("tensor", fs, waits=w, inc="cP" if last else None)
                        if last:
                            m["smc", q, kb] = r
                    if kb == kd:
                        co = q * 128 - kd * 512
                        def fm2(e, kb=kb, co=co):
                            return e.tensor_tensor(out=PS[:, kb * 512 + co: kb * 512 + co + 128],
                                                   in0=PS[:, kb * 512 + co: kb * 512 + co + 128],
                                                   in1=causal[:], op=ALU.add)
                        sch.op("vector", fm2, waits=[("cP", m["smc", q, kb]),
                                                     ("cG", m["mask"])], inc="cV")
                    def fpx(e, q=q, kb=kb, wdt=wdt):
                        return e.reduce_max(out=pmax[:, q % 2, kb:kb + 1],
                                            in_=PS[:, kb * 512: kb * 512 + wdt], axis=AX.X)
                    m["pmx", q, kb] = sch.op("vector", fpx,
                                             waits=[("cP", m["smc", q, kb])], inc="cV")
                m["smm", q] = m["smc", q, kbm - 1]
                def fx2(e, q=q, kbm=kbm):
                    return e.reduce_max(out=negmax[:, q:q + 1], in_=pmax[:, q % 2, 0:kbm],
                                        axis=AX.X, negate=True)
                m["max", q] = sch.op("vector", fx2, waits=[("cV", m["pmx", q, kbm - 1])],
                                     inc="cV")
        if q >= 2:
            p = q - 2
            peng = "vector" if p >= 6 else "scalar"
            emit_pt(p, 0)
            emit_pte(p, 0, peng)
            if p >= 1:
                emit_pt(p, 1)
                emit_pte(p, 1, peng)
        if q < NT:
            def fe2(e, q=q, Lk=Lk, sofs=sofs):
                return e.activation(out=P_sb[q % 3][:, 0:Lk], in_=PS[:, sofs:sofs + Lk],
                                    func=AF.Exp,
                                    bias=negmax[:, q:q + 1], accum_out=rsum[:, q:q + 1])
            m["exp", q] = sch.op("scalar", fe2, waits=[("cV", m["max", q])], inc="cA")
        if q >= 2:
            emit_ptpv_rest(q - 2)
        if 9 <= q <= 12:
            emit_xl2(q - 9)

    def emit_wdma(c, idx=(0, 1, 2, 3)):
        for i in idx:
            zc = c * 4 + i
            slot = WSL[c % 2][i]
            def wd(e, zc=zc, slot=slot):
                return e.dma_start(out=wbuf[:, slot, :], in_=wc_h[zc * 128:(zc + 1) * 128, :])
            w = []
            if c == 0 and i == 0:
                w = [("cP", m["kmm", 15])]
            elif c == 0 and i == 2:
                w = [("cP", m["pt", 13, 13])]
            elif c == 1 and i == 0:
                w = [("cP", m["pt", 14, 14])]
            elif c == 1 and i == 2:
                w = [("cP", m["pt", 15, 15])]
            elif c >= 2 and i == 0:
                w = [("cP", m["cpz", c - 2, 15])]
            sch.op("gpsimd", wd, waits=w, inc="dWB%d" % (c % 2), n=16)

    def emit_rails_one(c, j):
        tq, zi = j // 4, j % 4
        zc = c * 4 + zi
        hh = zc // 32
        cc = (zc % 32) // 4
        col = hh * 512 + (zc % 4) * 128
        bank = 6 + (j % 2)
        def rm_(e, cc=cc, col=col, bank=bank, tq=tq):
            return e.matmul(PS[:, bank * 512:(bank + 1) * 512], railw[:, cc, col:col + 128],
                            xl_fm[:, cc, tq * 512:(tq + 1) * 512], start=True, stop=True)
        w = []
        if j == 0:
            w.append(m["mde", c])
        if c == 0:
            if zi == 0:
                w.append(m["lnTe2", 4 * tq + 3])
            if j >= 2:
                w.append(("cA", m["gelu", 0, j - 2]))
        elif j >= 2:
            w.append(("cA", m["gelu", c, j - 2]))
        else:
            w.append(("cA", m["gelu", c - 1, 14 + j]))
        m["rmm", c, j] = sch.op("tensor", rm_, waits=w, inc="cP")
        def gl(e, c=c, zi=zi, tq=tq, bank=bank):
            return e.activation(out=h_sl[c % 2][:, zi, tq * 512:(tq + 1) * 512],
                                in_=PS[:, bank * 512:(bank + 1) * 512], func=AF.Gelu)
        w = [("cP", m["rmm", c, j])]
        if c >= 2 and j == 0:
            w.append(("cP", m["cpz", c - 2, 15]))
        m["gelu", c, j] = sch.op("scalar", gl, waits=w, inc="cA")

    def emit_cproj_one(c, tt):
        g_ = c * 16 + tt
        p = g_ % 3
        for zi in range(4):
            for eh in range(2):
                def cm(e, c=c, tt=tt, zi=zi, eh=eh, p=p):
                    return e.matmul(PS[:, p * 1024 + eh * 512: p * 1024 + eh * 512 + 512],
                                    h_sl[c % 2][:, zi, tt * 128:(tt + 1) * 128],
                                    wbuf[:, WSL[c % 2][zi], eh * 512:(eh + 1) * 512],
                                    start=(zi == 0), stop=(zi == 3))
                w = []
                if zi == 0 and eh == 0:
                    w = [("dWB%d" % (c % 2), 16 * 4 * (c // 2 + 1)),
                         ("cA", m["gelu", c, (tt // 4) * 4 + 3])]
                    if g_ >= 3:
                        w.append(("cV", m["cpe", (g_ - 3) // 16, (g_ - 3) % 16]))
                    if g_ == 0:
                        w += [m["mde", 30], m["mde", 31], m["lnTe2", NT - 1]]
                last = (zi == 3 and eh == 1)
                r = sch.op("tensor", cm, waits=w, inc="cP" if last else None)
                if last:
                    m["cpz", c, tt] = r
        def ev(e, tt=tt, p=p):
            return e.tensor_tensor(out=x_t[:, tt, :], in0=PS[:, p * 1024:p * 1024 + 1024],
                                   in1=x_t[:, tt, :], op=ALU.add)
        m["cpe", c, tt] = sch.op("vector", ev, waits=[("cP", m["cpz", c, tt])], inc="cV")
        if c == NCK - 1:
            def yd(e, tt=tt):
                return e.dma_start(out=y_h[tt * 128:(tt + 1) * 128, :], in_=x_t[:, tt, :])
            sch.op("sync", yd, waits=[("cV", m["cpe", c, tt])], inc="dY", n=16)

    for q in range(NT):
        if 2 * q + 4 < 32:
            emit_md(2 * q + 4)
            emit_md(2 * q + 5)
        if q == 10:
            emit_wdma(0, (2, 3))
        if q >= 4:
            emit_xl2(q)
        po = (4 + q % 2) * 1024
        for c in range(NCH):
            def gt(e, q=q, c=c, po=po):
                return e.transpose(PSb[:, po + c * 128: po + c * 128 + 128],
                                   xlroll[:, q % 4, c * 128:c * 128 + 128], eye[:])
            w = W(m["xl2", q] if c == 0 else None,
                  ("cV", m["x2", NT - 1]) if q == 0 and c == 0 else None,
                  m["pte", NT - 1, NT - 1] if q == 1 and c == 0 else None,
                  m.get(("lnTe2", q - 2)) if q >= 2 and c == 0 else None)
            r = sch.op("tensor", gt, waits=w, inc="cP" if c == NCH - 1 else None)
            if c == NCH - 1:
                m["lnT2", q] = r
        def ge(e, q=q, po=po):
            src = PSb[:, po:po + 1024].rearrange("p (c k) -> p c k", k=128)
            return e.tensor_copy(out=xl_fm[:, :, q * 128:q * 128 + 128], in_=src)
        m["lnTe2", q] = ("cV", sch.op("vector", ge, waits=[("cP", m["lnT2", q])], inc="cV"))
        if q % 4 == 3:
            for j in range((q - 3), q + 1):
                emit_rails_one(0, j)
    x2l_fm = xl_fm

    for c in range(1, NCK):
        emit_wdma(c)
        for tt in range(16):
            emit_cproj_one(c - 1, tt)
            emit_rails_one(c, tt)
    for tt in range(16):
        emit_cproj_one(NCK - 1, tt)

    def fin(e):
        return e.nop()
    sch.op("sync", fin, waits=[("dY", 16 * 16)])

    with es, nc.Block() as blk:
        decos = {"sync": blk.sync, "tensor": blk.tensor, "vector": blk.vector,
                 "scalar": blk.scalar, "gpsimd": blk.gpsimd}
        for name in ENGS:
            def make(name):
                def emit(eng):
                    seen = {}
                    for waits, fn, inc, n in sch.ops[name]:
                        for s, v in waits:
                            if seen.get(s, 0) < v:
                                eng.wait_ge(sems[s], v)
                                seen[s] = v
                        inst = fn(eng)
                        if inc is not None:
                            inst.then_inc(sems[inc], n)
                return emit
            decos[name](make(name))
    return nc


def _host_prep(L0_attn_x, L0_attn_W, L0_ffn_rail_maker, L0_ffn_rail_up_proj, L0_ffn_w_cproj):
    x = np.ascontiguousarray(np.asarray(L0_attn_x))
    Wm = np.asarray(L0_attn_W)
    rm = np.asarray(L0_ffn_rail_maker)
    rup = np.asarray(L0_ffn_rail_up_proj)
    wc = np.asarray(L0_ffn_w_cproj)
    wkT = np.ascontiguousarray(Wm[:D].T)
    wvT = np.ascontiguousarray(Wm[D:].T)
    wcT = np.ascontiguousarray(wc.T)
    A = np.zeros((4, 8, 128, 128), np.float32)
    B = np.zeros((4, 8, 128, 512), np.float32)
    for h in range(4):
        for c in range(8):
            for j in range(8):
                r = 8 * c + j
                A[h, c, 16 * j:16 * j + 16, 16 * j:16 * j + 16] = rm[h, r].T
                B[h, c, 16 * j:16 * j + 16, 64 * j:64 * j + 64] = rup[h, r].T
    At = A.reshape(32, 128, 128)
    Ap = np.concatenate([np.concatenate([At[8 * t + k] for k in range(8)], axis=1)
                         for t in range(4)], axis=0)
    Bt = B.reshape(32, 128, 512)
    Bp = np.concatenate([np.concatenate([Bt[2 * t], Bt[2 * t + 1]], axis=1)
                         for t in range(16)], axis=0)
    abp = np.ascontiguousarray(np.concatenate([Ap, Bp], axis=0))
    return x, wkT, wvT, abp, wcT


def kernel(L0_attn_x, L0_attn_W, L0_ffn_rail_maker, L0_ffn_rail_up_proj, L0_ffn_w_cproj):
    x, wkT, wvT, abp, wcT = _host_prep(L0_attn_x, L0_attn_W, L0_ffn_rail_maker,
                                       L0_ffn_rail_up_proj, L0_ffn_w_cproj)
    nc = build_nc()
    in_maps = [{"x": np.ascontiguousarray(x[b]), "wkT": wkT, "wvT": wvT,
                "abp": abp, "wcT": wcT} for b in range(8)]
    res = run_bass_kernel_spmd(nc, in_maps, core_ids=list(range(8)))
    return np.stack([np.asarray(r["y"]) for r in res.results], axis=0)


# version 42 (final)
# speedup vs baseline: 1.1972x; 1.1972x over previous
import numpy as np
import concourse.bass as bass
import concourse.mybir as mybir
from concourse.bass_utils import run_bass_kernel_spmd
from contextlib import ExitStack

F32 = mybir.dt.float32
BF16 = mybir.dt.bfloat16
U8 = mybir.dt.uint8
AF = mybir.ActivationFunctionType
ALU = mybir.AluOpType
AX = mybir.AxisListType

T, D = 2048, 1024
NT, NCH = 16, 8
NCK = 32
NEG = -30000.0
EPS = 1e-5

ENGS = ("sync", "tensor", "vector", "scalar", "gpsimd")


class Sch:

    def __init__(self, sem_names):
        self.ops = {e: [] for e in ENGS}
        self.cnt = {s: 0 for s in sem_names}

    def op(self, eng, fn, waits=(), inc=None, n=1):
        for s, v in waits:
            assert v <= self.cnt[s], (s, v, self.cnt[s])
        self.ops[eng].append((tuple((s, v) for s, v in waits if v > 0), fn, inc, n))
        if inc is not None:
            self.cnt[inc] += n
            return self.cnt[inc]
        return None


def build_nc():
    nc = bass.Bass("TRN2", target_bir_lowering=False)
    x_h = nc.dram_tensor("x", [T, D], F32, kind="ExternalInput").ap()
    wkT_h = nc.dram_tensor("wkT", [D, D], F32, kind="ExternalInput").ap()
    wvT_h = nc.dram_tensor("wvT", [D, D], F32, kind="ExternalInput").ap()
    abp_h = nc.dram_tensor("abp", [2560, D], F32, kind="ExternalInput").ap()
    wc_h = nc.dram_tensor("wcT", [16384, D], F32, kind="ExternalInput").ap()
    y_h = nc.dram_tensor("y", [T, D], F32, kind="ExternalOutput").ap()

    es = ExitStack()
    ARENA = 212800
    arena = es.enter_context(nc.sbuf_tensor("arena", [128, ARENA], U8))
    base = nc.lookup_mloc(arena).addr

    def sb(name, shape, dtype, off):
        return nc.alloc_sbuf_tensor_at(name, shape, dtype, offset=base + off)

    x_t = sb("x_t", [128, NT, 1024], F32, 0)
    eye = sb("eye", [128, 128], BF16, 65536)
    colc = sb("colc", [128, 128], F32, 66048)
    rowi = sb("rowi", [128, 1], F32, 66560)
    bnst = sb("bnst", [128, 2, 12], F32, 66592)
    mv = sb("mv", [128, 16, 2], F32, 66720)
    tmpln = sb("tmpln", [128, 16], F32, 66848)
    rstd = sb("rstd", [128, 16], F32, 66912)
    negmean = sb("negmean", [128, 16], F32, 66976)
    nmr = sb("nmr", [128, 16], F32, 67040)
    t1s = sb("t1s", [128, 16], F32, 67104)
    negmax = sb("negmax", [128, 16], F32, 67168)
    rsum = sb("rsum", [128, 16], F32, 67232)
    rl = sb("rl", [128, 16], F32, 67296)
    ceps = sb("ceps", [128, 1], F32, 67360)
    chalf = sb("chalf", [128, 1], F32, 67392)
    cneg1 = sb("cneg1", [128, 1], F32, 67424)
    cone = sb("cone", [128, 1], F32, 67456)
    pmax = sb("pmax", [128, 2, 4], F32, 67488)
    PT_s = sb("PT_s", [128, 2, 128], BF16, 67520)
    wk_bf = sb("wk_bf", [128, 8, 1024], BF16, 68096)
    wv_bf = sb("wv_bf", [128, 8, 1024], BF16, 84480)
    P_sb = [sb("P_sa", [128, 2048], BF16, 68096),
            sb("P_sc", [128, 2048], BF16, 72192),
            sb("P_se", [128, 2048], BF16, 76288)]
    wbuf = sb("wbuf", [128, 8, 1024], BF16, 68096)
    WSL = ((6, 7, 2, 3), (4, 5, 0, 1))
    ab_b2 = sb("ab_b2", [128, 2, 1024], BF16, 207360)
    ab_b2b = sb("ab_b2b", [128, 2, 1024], BF16, 92672)
    abA = sb("abA", [128, 4, 1024], BF16, 84480)
    xlroll = sb("xlroll", [128, 4, 1024], BF16, 100864)
    xl_fm = sb("xl_fm", [128, NCH, 2048], BF16, 109056)
    kf = sb("kf", [128, NCH, 2048], BF16, 141824)
    railw = kf
    vt = sb("vt", [128, NT, 1024], BF16, 174592)
    h_sl = [sb("h_a", [128, 4, 2048], BF16, 174592),
            sb("h_b", [128, 4, 2048], BF16, 190976)]

    PS = nc.alloc_psum_tensor("PS", [128, 4096], F32).ap()
    PSb = PS.bitcast(BF16)

    sem_names = (["dX", "dX2", "dT0", "dT1", "dT2", "dT3", "dWK",
                  "dAB0", "dB0", "dB1", "dB2", "dB3", "dWB0", "dWB1", "dY",
                  "cP", "cV", "cA", "cG"] + ["dV%d" % c for c in range(16)])
    sems = {s: es.enter_context(nc.semaphore(s)) for s in sem_names}
    sch = Sch(sem_names)
    m = {}

    def W(*pairs):
        return [p for p in pairs if p is not None and p[1] is not None]

    nc.const_aps.register(F32, EPS, ceps.ap())
    nc.const_aps.register(F32, -0.5, chalf.ap())
    nc.const_aps.register(F32, -1.0, cneg1.ap())
    nc.const_aps.register(F32, 1.0, cone.ap())
    sch.op("gpsimd", lambda e: e.memset(ceps[:], EPS))
    sch.op("gpsimd", lambda e: e.memset(chalf[:], -0.5))
    sch.op("gpsimd", lambda e: e.memset(cneg1[:], -1.0))
    m["consts"] = sch.op("gpsimd", lambda e: e.memset(cone[:], 1.0), inc="cG")
    sch.op("gpsimd", lambda e: e.iota(colc[:], pattern=[[1, 128]], channel_multiplier=0,
                                      allow_small_or_imprecise_dtypes=True))
    m["iota"] = sch.op("gpsimd", lambda e: e.iota(rowi[:], pattern=[[0, 1]], channel_multiplier=1,
                                                  allow_small_or_imprecise_dtypes=True), inc="cG")
    m["eye"] = sch.op("vector", lambda e: e.tensor_scalar(out=eye[:], in0=colc[:], scalar1=rowi[:],
                                                          scalar2=None, op0=ALU.is_equal),
                      waits=[("cG", m["iota"])], inc="cV")
    m["mz"] = sch.op("gpsimd", lambda e: e.memset(colc[:], 0.0), waits=[("cV", m["eye"])], inc="cG")
    m["mask"] = sch.op("gpsimd", lambda e: e.affine_select(out=colc[:], in_=colc[:],
                                                           pattern=[[-1, 128]], compare_op=ALU.is_ge,
                                                           fill=NEG, channel_multiplier=1),
                       waits=[("cG", m["mz"])], inc="cG")
    causal = colc

    for q in range(NT):
        eng = "sync" if q % 2 == 0 else "scalar"
        sem = f"dT{q}" if q < 4 else ("dX" if q % 2 == 0 else "dX2")
        def f(e, q=q):
            return e.dma_start(out=x_t[:, q, :], in_=x_h[q * 128:(q + 1) * 128, :])
        sch.op(eng, f, inc=sem, n=16)
    for q in range(NT):
        sem = f"dT{q}" if q < 4 else ("dX" if q % 2 == 0 else "dX2")
        m["dx", q] = (sem, sch.cnt[sem])

    for i in range(16):
        c2, h = i // 2, i % 2
        def fdma(e, c2=c2, h=h):
            return e.dma_start(out=wv_bf[:, c2, h * 512:(h + 1) * 512],
                               in_=wvT_h[c2 * 128:(c2 + 1) * 128, h * 512:(h + 1) * 512])
        sch.op("gpsimd", fdma, inc="dV%d" % i, n=16)
    for c2 in range(8):
        def fkdma(e, c2=c2):
            return e.dma_start(out=wk_bf[:, c2, :], in_=wkT_h[c2 * 128:(c2 + 1) * 128, :])
        sch.op("gpsimd", fkdma, inc="dWK", n=16)
    m["wk_all"] = ("dWK", sch.cnt["dWK"])

    def emit_v(qt):
        po = 2048 + (qt % 2) * 1024
        for c in range(NCH):
            for eh in range(2):
                def fv(e, qt=qt, c=c, eh=eh, po=po):
                    return e.matmul(PS[:, po + eh * 512: po + eh * 512 + 512],
                                    xl_fm[:, c, qt * 128:(qt + 1) * 128],
                                    wv_bf[:, c, eh * 512:(eh + 1) * 512],
                                    start=(c == 0), stop=(c == NCH - 1))
                w = []
                if c == 0 and eh == 0:
                    w = W(("cA", m["lnTe", qt]),
                          ("cV", m.get(("ve", qt - 2))) if qt >= 2 else None)
                if qt <= 1:
                    w.append(("dV%d" % (2 * c + eh), 16))
                last = (c == NCH - 1 and eh == 1)
                r = sch.op("tensor", fv, waits=w, inc="cP" if last else None)
                if last:
                    m["vmm", qt] = r

    def emit_ve(qt):
        po = 2048 + (qt % 2) * 1024
        def fve(e, qt=qt, po=po):
            return e.tensor_copy(out=vt[:, qt, :], in_=PS[:, po:po + 1024])
        m["ve", qt] = sch.op("vector", fve, waits=[("cP", m["vmm", qt])], inc="cV")

    for q in range(NT):
        def f1(e, q=q):
            return e.bn_stats(out=bnst[:, q % 2, 0:6], in_=x_t[:, q, 0:512])
        def f2(e, q=q):
            return e.bn_stats(out=bnst[:, q % 2, 6:12], in_=x_t[:, q, 512:1024])
        def f3(e, q=q):
            return e.bn_aggr(out=mv[:, q, :], in_=bnst[:, q % 2, :])
        sch.op("vector", f1, waits=[m["dx", q]], inc="cV")
        m["bns", q] = sch.op("vector", f2, inc="cV")
        m["aggr", q] = sch.op("vector", f3, waits=[("cV", m["bns", q])], inc="cV")
        def f4(e, q=q):
            return e.activation(out=tmpln[:, q:q + 1], in_=mv[:, q, 1:2], func=AF.Ln, bias=EPS)
        m["ln", q] = sch.op("scalar", f4, waits=W(("cV", m["aggr", q]),
                                                  ("cG", m["consts"]) if q == 0 else None), inc="cA")
        def f5(e, q=q):
            return e.activation(out=rstd[:, q:q + 1], in_=tmpln[:, q:q + 1], func=AF.Exp, scale=-0.5)
        m["rstd", q] = sch.op("scalar", f5, waits=[("cA", m["ln", q])], inc="cA")
        def f6(e, q=q):
            return e.activation(out=negmean[:, q:q + 1], in_=mv[:, q, 0:1], func=AF.Identity,
                                scale=-1.0)
        m["ngm", q] = sch.op("scalar", f6, inc="cA")
        def f7(e, q=q):
            return e.activation(out=nmr[:, q:q + 1], in_=negmean[:, q:q + 1], func=AF.Identity,
                                scale=rstd[:, q:q + 1])
        m["nmr1", q] = sch.op("scalar", f7, waits=[("cA", m["ngm", q])], inc="cA")
        def f8(e, q=q):
            return e.activation(out=t1s[:, q:q + 1], in_=rstd[:, q:q + 1], func=AF.Identity,
                                bias=1.0)
        sch.op("scalar", f8, waits=[("cA", m["rstd", q])])
        def f9(e, q=q):
            return e.activation(out=xlroll[:, q % 2, :], in_=x_t[:, q, :], func=AF.Identity,
                                scale=rstd[:, q:q + 1], bias=nmr[:, q:q + 1])
        m["xl", q] = sch.op("scalar", f9, waits=[("cA", m["nmr1", q])], inc="cA")
        def f10(e, q=q):
            return e.tensor_scalar(out=x_t[:, q, :], in0=x_t[:, q, :], scalar1=t1s[:, q:q + 1],
                                   scalar2=nmr[:, q:q + 1], op0=ALU.mult, op1=ALU.add)
        m["t1", q] = sch.op("gpsimd", f10, waits=[("cA", m["xl", q])], inc="cG")
        po = (q % 2) * 1024
        for c in range(NCH):
            def ft(e, q=q, c=c, po=po):
                return e.transpose(PSb[:, po + c * 128: po + c * 128 + 128],
                                   xlroll[:, q % 2, c * 128:c * 128 + 128], eye[:])
            w = W(("cA", m["xl", q]) if c == 0 else None,
                  ("cV", m["eye"]) if q == 0 and c == 0 else None,
                  ("cA", m.get(("lnTe", q - 2))) if q >= 2 and c == 0 else None)
            r = sch.op("tensor", ft, waits=w, inc="cP" if c == NCH - 1 else None)
            if c == NCH - 1:
                m["lnT", q] = r
        def fe(e, q=q, po=po):
            src = PSb[:, po:po + 1024].rearrange("p (c k) -> p c k", k=128)
            return e.copy(out=xl_fm[:, :, q * 128:q * 128 + 128], in_=src)
        m["lnTe", q] = sch.op("scalar", fe, waits=[("cP", m["lnT", q])], inc="cA")
        if q >= 1:
            emit_v(q - 1)
        if q >= 2:
            emit_ve(q - 2)
    emit_v(NT - 1)
    emit_ve(NT - 2)
    emit_ve(NT - 1)
    m["ve_all"] = m["ve", NT - 1]

    ii = 0
    for j in range(NCH):
        for th in range(2):
            po = ((2 * j + th) % 2) * 1024
            for c in range(NCH):
                for kbb in range(2):
                    def fk(e, j=j, th=th, c=c, kbb=kbb, po=po):
                        return e.matmul(PS[:, po + kbb * 512: po + kbb * 512 + 512],
                                        wk_bf[:, c, j * 128:(j + 1) * 128],
                                        xl_fm[:, c, th * 1024 + kbb * 512: th * 1024 + kbb * 512 + 512],
                                        start=(c == 0), stop=(c == NCH - 1))
                    w = []
                    if ii == 0:
                        w = [("cA", m["lnTe", NT - 1]), m["wk_all"]]
                    elif c == 0 and kbb == 0 and ii >= 2:
                        w = W(("cA", m.get(("ke", ii - 2))))
                    last = (c == NCH - 1 and kbb == 1)
                    r = sch.op("tensor", fk, waits=w, inc="cP" if last else None)
                    if last:
                        m["kmm", ii] = r
            def fke(e, j=j, th=th, po=po):
                return e.copy(out=kf[:, j, th * 1024:(th + 1) * 1024], in_=PS[:, po:po + 1024])
            m["ke", ii] = sch.op("scalar", fke, waits=[("cP", m["kmm", ii])], inc="cA")
            ii += 1
    m["ke_all"] = m["ke", ii - 1]

    for t in range(4):
        def ha(e, t=t):
            return e.dma_start(out=abA[:, t, :], in_=abp_h[128 * t:128 * t + 128, :])
        w = [("cP", m["vmm", NT - 1])] if t == 0 else []
        sch.op("gpsimd", ha, waits=w, inc="dAB0", n=16)
    m["aball"] = sch.cnt["dAB0"]
    for i in range(2):
        def wd0(e, zc=i, slot=WSL[0][i]):
            return e.dma_start(out=wbuf[:, slot, :], in_=wc_h[zc * 128:(zc + 1) * 128, :])
        w = [("cP", m["kmm", 15])] if i == 0 else []
        sch.op("gpsimd", wd0, waits=w, inc="dWB0", n=16)

    def bsl(j):
        return ab_b2[:, j % 4, :] if j % 4 < 2 else ab_b2b[:, j % 4 - 2, :]

    def emit_md(i):
        if i % 2 == 0:
            j = i // 2
            def hb(e, j=j):
                return e.dma_start(out=bsl(j), in_=abp_h[512 + 128 * j:512 + 128 * j + 128, :])
            w = W(("cP", m.get(("md", 2 * j - 7))))
            bsem = "dB%d" % (j % 4)
            sch.op("gpsimd", hb, waits=w, inc=bsem, n=16)
            m["bload", j] = (bsem, sch.cnt[bsem])
        hh, c = i // 8, i % 8
        po = (i % 2) * 512
        def hm(e, po=po, i=i):
            return e.matmul(PS[:, po:po + 512], abA[:, i // 8, (i % 8) * 128:(i % 8) * 128 + 128],
                            bsl(i // 2)[:, (i % 2) * 512:(i % 2) * 512 + 512],
                            start=True, stop=True)
        w = [m["bload", i // 2]]
        if i == 0:
            w += [("dAB0", m["aball"]), ("cA", m["exp", NT - 1])]
        if i >= 2:
            w.append(m["mde", i - 2])
        m["md", i] = sch.op("tensor", hm, waits=w, inc="cP")
        if i % 2 == 0:
            def he(e, hh=hh, c=c, po=po):
                return e.copy(out=railw[:, c, hh * 512:(hh + 1) * 512], in_=PS[:, po:po + 512])
            eng, sem = "scalar", "cA"
        else:
            def he(e, hh=hh, c=c, po=po):
                return e.tensor_copy(out=railw[:, c, hh * 512:(hh + 1) * 512],
                                     in_=PS[:, po:po + 512])
            eng, sem = "vector", "cV"
        w = [("cP", m["md", i])]
        if i == 0:
            w.append(("cP", m["smm", NT - 1]))
        m["mde", i] = (sem, sch.op(eng, he, waits=w, inc=sem))

    def emit_xl2(q):
        def g7(e, q=q):
            return e.tensor_scalar(out=xlroll[:, q % 4, :], in0=x_t[:, q, :],
                                   scalar1=rstd[:, q:q + 1], scalar2=nmr[:, q:q + 1],
                                   op0=ALU.mult, op1=ALU.add)
        w = [("cA", m["nm2", q])]
        if q >= 4:
            w.append(("cP", m["lnT2", q - 4]))
        m["xl2", q] = ("cG", sch.op("gpsimd", g7, waits=w, inc="cG"))

    def emit_pt(q, kt):
        po = 6144 + (kt % 2) * 1024
        def ft2(e, q=q, kt=kt, po=po):
            return e.transpose(PSb[:, po:po + 128], P_sb[q % 3][:, kt * 128:(kt + 1) * 128],
                               eye[:])
        w = W(("cA", m["exp", q]) if kt <= 1 else None,
              ("cV", m["ve_all"]) if q == 0 and kt <= 1 else None,
              m.get(("pte", q, kt - 2)) if kt >= 2 else None,
              ("cV", m.get(("x2", q - 1))) if q >= 1 and kt <= 1 else None)
        m["pt", q, kt] = sch.op("tensor", ft2, waits=w, inc="cP")

    def emit_pte(q, kt, eng):
        po = 6144 + (kt % 2) * 1024
        if eng == "scalar":
            def fpe(e, q=q, kt=kt, po=po):
                return e.copy(out=PT_s[:, kt % 2, :], in_=PSb[:, po:po + 128])
        else:
            def fpe(e, q=q, kt=kt, po=po):
                return e.tensor_copy(out=PT_s[:, kt % 2, :], in_=PSb[:, po:po + 128])
        sem = {"scalar": "cA", "vector": "cV"}[eng]
        m["pte", q, kt] = (sem, sch.op(eng, fpe, waits=[("cP", m["pt", q, kt])], inc=sem))

    def emit_pv(q, kt):
        for eh in range(2):
            def fpv(e, q=q, kt=kt, eh=eh):
                return e.matmul(PS[:, 2048 + eh * 512: 2048 + eh * 512 + 512],
                                PT_s[:, kt % 2, :], vt[:, kt, eh * 512:(eh + 1) * 512],
                                start=(kt == 0), stop=(kt == q))
            w = []
            if eh == 0:
                w = W(m["pte", q, kt],
                      ("cV", m.get(("x2", q - 1))) if q >= 1 and kt == 0 else None)
            r = sch.op("tensor", fpv, waits=w, inc="cP" if eh == 1 else None)
            if eh == 1:
                m["pv", q, kt] = r

    def emit_ptpv_rest(q):
        emit_pv(q, 0)
        for kt in range(2, q + 1):
            emit_pt(q, kt)
            emit_pte(q, kt, "scalar" if kt >= 6 else "vector")
            emit_pv(q, kt - 1)
            if q == NT - 1 and kt in (6, 10):
                i0 = 0 if kt == 6 else 2
                emit_md(i0)
                emit_md(i0 + 1)
        if q >= 1:
            emit_pv(q, q)
        def fr(e, q=q):
            return e.reciprocal(out=rl[:, q:q + 1], in_=rsum[:, q:q + 1])
        m["rl", q] = sch.op("vector", fr, waits=[("cA", m["exp", q])], inc="cV")
        def f11(e, q=q):
            return e.scalar_tensor_tensor(out=x_t[:, q, :], in0=PS[:, 2048:3072],
                                          scalar=rl[:, q:q + 1], in1=x_t[:, q, :],
                                          op0=ALU.mult, op1=ALU.add)
        m["x2", q] = sch.op("vector", f11,
                            waits=[("cP", m["pv", q, q]), ("cG", m["t1", q]),
                                   ("cV", m["rl", q])], inc="cV")
        def g1(e, q=q):
            return e.bn_stats(out=bnst[:, q % 2, 0:6], in_=x_t[:, q, 0:512])
        def g2(e, q=q):
            return e.bn_stats(out=bnst[:, q % 2, 6:12], in_=x_t[:, q, 512:1024])
        def g3(e, q=q):
            return e.bn_aggr(out=mv[:, q, :], in_=bnst[:, q % 2, :])
        sch.op("vector", g1, waits=[("cV", m["x2", q])], inc="cV")
        m["bns2", q] = sch.op("vector", g2, inc="cV")
        m["aggr2", q] = sch.op("vector", g3, waits=[("cV", m["bns2", q])], inc="cV")
        def g4(e, q=q):
            return e.activation(out=tmpln[:, q:q + 1], in_=mv[:, q, 1:2], func=AF.Ln, bias=EPS)
        m["ln2", q] = sch.op("scalar", g4, waits=[("cV", m["aggr2", q])], inc="cA")
        def g5(e, q=q):
            return e.activation(out=rstd[:, q:q + 1], in_=tmpln[:, q:q + 1], func=AF.Exp, scale=-0.5)
        m["rstd2", q] = sch.op("scalar", g5, waits=[("cA", m["ln2", q])], inc="cA")
        def g6(e, q=q):
            return e.activation(out=negmean[:, q:q + 1], in_=mv[:, q, 0:1], func=AF.Identity,
                                scale=-1.0)
        m["ngm2", q] = sch.op("scalar", g6, inc="cA")
        def g6b(e, q=q):
            return e.activation(out=nmr[:, q:q + 1], in_=negmean[:, q:q + 1], func=AF.Identity,
                                scale=rstd[:, q:q + 1])
        m["nm2", q] = sch.op("scalar", g6b, waits=[("cA", m["ngm2", q])], inc="cA")

    for q in range(NT + 2):
        if q < NT:
            Lk = (q + 1) * 128
            kbm = (Lk + 511) // 512
            sofs = (q % 2) * 1024 if q <= 7 else 0
            if q <= 7:
                for c in range(NCH):
                    for kb in range(kbm):
                        wdt = min(512, Lk - kb * 512)
                        def fs(e, q=q, c=c, kb=kb, wdt=wdt, sofs=sofs):
                            return e.matmul(PS[:, sofs + kb * 512: sofs + kb * 512 + wdt],
                                            xl_fm[:, c, q * 128:(q + 1) * 128],
                                            kf[:, c, kb * 512: kb * 512 + wdt],
                                            start=(c == 0), stop=(c == NCH - 1))
                        w = []
                        if c == 0 and kb == 0:
                            if q == 0:
                                w = [("cA", m["ke", 14])]
                            elif q == 1:
                                w = [("cA", m["ke_all"])]
                            else:
                                w = [("cA", m["exp", q - 2])]
                        last = (c == NCH - 1 and kb == kbm - 1)
                        r = sch.op("tensor", fs, waits=w, inc="cP" if last else None)
                        if last:
                            m["smm", q] = r
                def fm(e, q=q, sofs=sofs):
                    return e.tensor_tensor(out=PS[:, sofs + q * 128: sofs + (q + 1) * 128],
                                           in0=PS[:, sofs + q * 128: sofs + (q + 1) * 128],
                                           in1=causal[:], op=ALU.add)
                m["msk", q] = sch.op("vector", fm, waits=[("cP", m["smm", q]),
                                                          ("cG", m["mask"])], inc="cV")
                def fx(e, q=q, Lk=Lk, sofs=sofs):
                    return e.reduce_max(out=negmax[:, q:q + 1], in_=PS[:, sofs:sofs + Lk],
                                        axis=AX.X, negate=True)
                m["max", q] = sch.op("vector", fx, waits=[("cV", m["msk", q])], inc="cV")
            else:
                kd = (q * 128) // 512
                for kb in range(kbm):
                    wdt = min(512, Lk - kb * 512)
                    for c in range(NCH):
                        def fs(e, q=q, c=c, kb=kb, wdt=wdt):
                            return e.matmul(PS[:, kb * 512: kb * 512 + wdt],
                                            xl_fm[:, c, q * 128:(q + 1) * 128],
                                            kf[:, c, kb * 512: kb * 512 + wdt],
                                            start=(c == 0), stop=(c == NCH - 1))
                        w = []
                        if c == 0 and kb == 0:
                            w = [("cA", m["exp", 7 if q == 8 else q - 1])]
                        last = (c == NCH - 1)
                        r = sch.op("tensor", fs, waits=w, inc="cP" if last else None)
                        if last:
                            m["smc", q, kb] = r
                    if kb == kd:
                        co = q * 128 - kd * 512
                        def fm2(e, kb=kb, co=co):
                            return e.tensor_tensor(out=PS[:, kb * 512 + co: kb * 512 + co + 128],
                                                   in0=PS[:, kb * 512 + co: kb * 512 + co + 128],
                                                   in1=causal[:], op=ALU.add)
                        sch.op("vector", fm2, waits=[("cP", m["smc", q, kb]),
                                                     ("cG", m["mask"])], inc="cV")
                    def fpx(e, q=q, kb=kb, wdt=wdt):
                        return e.reduce_max(out=pmax[:, q % 2, kb:kb + 1],
                                            in_=PS[:, kb * 512: kb * 512 + wdt], axis=AX.X)
                    m["pmx", q, kb] = sch.op("vector", fpx,
                                             waits=[("cP", m["smc", q, kb])], inc="cV")
                    if kb == 1:
                        emit_pt(q - 2, 0)
                        emit_pte(q - 2, 0, "vector")
                    elif kb == 2:
                        emit_pt(q - 2, 1)
                        emit_pte(q - 2, 1, "vector")
                m["smm", q] = m["smc", q, kbm - 1]
                def fx2(e, q=q, kbm=kbm):
                    return e.reduce_max(out=negmax[:, q:q + 1], in_=pmax[:, q % 2, 0:kbm],
                                        axis=AX.X, negate=True)
                m["max", q] = sch.op("vector", fx2, waits=[("cV", m["pmx", q, kbm - 1])],
                                     inc="cV")
        if q >= 2 and (q <= 7 or q >= NT):
            p = q - 2
            peng = "vector" if p >= 6 else "scalar"
            emit_pt(p, 0)
            emit_pte(p, 0, peng)
            if p >= 1:
                emit_pt(p, 1)
                emit_pte(p, 1, peng)
        if q < NT:
            def fe2(e, q=q, Lk=Lk, sofs=sofs):
                return e.activation(out=P_sb[q % 3][:, 0:Lk], in_=PS[:, sofs:sofs + Lk],
                                    func=AF.Exp,
                                    bias=negmax[:, q:q + 1], accum_out=rsum[:, q:q + 1])
            m["exp", q] = sch.op("scalar", fe2, waits=[("cV", m["max", q])], inc="cA")
        if q >= 2:
            emit_ptpv_rest(q - 2)
        if 9 <= q <= 12:
            emit_xl2(q - 9)

    def emit_wdma(c, idx=(0, 1, 2, 3)):
        for i in idx:
            zc = c * 4 + i
            slot = WSL[c % 2][i]
            def wd(e, zc=zc, slot=slot):
                return e.dma_start(out=wbuf[:, slot, :], in_=wc_h[zc * 128:(zc + 1) * 128, :])
            w = []
            if c == 0 and i == 0:
                w = [("cP", m["kmm", 15])]
            elif c == 0 and i == 2:
                w = [("cP", m["pt", 13, 13])]
            elif c == 1 and i == 0:
                w = [("cP", m["pt", 14, 14])]
            elif c == 1 and i == 2:
                w = [("cP", m["pt", 15, 15])]
            elif c >= 2 and i == 0:
                w = [("cP", m["cpz", c - 2, 15])]
            sch.op("gpsimd", wd, waits=w, inc="dWB%d" % (c % 2), n=16)

    def emit_rails_one(c, j):
        tq, zi = j // 4, j % 4
        zc = c * 4 + zi
        hh = zc // 32
        cc = (zc % 32) // 4
        col = hh * 512 + (zc % 4) * 128
        bank = 6 + (j % 2)
        def rm_(e, cc=cc, col=col, bank=bank, tq=tq):
            return e.matmul(PS[:, bank * 512:(bank + 1) * 512], railw[:, cc, col:col + 128],
                            xl_fm[:, cc, tq * 512:(tq + 1) * 512], start=True, stop=True)
        w = []
        if j == 0:
            w.append(m["mde", c])
        if c == 0:
            if zi == 0:
                w.append(m["lnTe2", 4 * tq + 3])
            if j >= 2:
                w.append(("cA", m["gelu", 0, j - 2]))
        elif j >= 2:
            w.append(("cA", m["gelu", c, j - 2]))
        else:
            w.append(("cA", m["gelu", c - 1, 14 + j]))
        m["rmm", c, j] = sch.op("tensor", rm_, waits=w, inc="cP")
        def gl(e, c=c, zi=zi, tq=tq, bank=bank):
            return e.activation(out=h_sl[c % 2][:, zi, tq * 512:(tq + 1) * 512],
                                in_=PS[:, bank * 512:(bank + 1) * 512], func=AF.Gelu)
        w = [("cP", m["rmm", c, j])]
        if c >= 2 and j == 0:
            w.append(("cP", m["cpz", c - 2, 15]))
        m["gelu", c, j] = sch.op("scalar", gl, waits=w, inc="cA")

    def emit_cproj_one(c, tt):
        g_ = c * 16 + tt
        p = g_ % 3
        for zi in range(4):
            for eh in range(2):
                def cm(e, c=c, tt=tt, zi=zi, eh=eh, p=p):
                    return e.matmul(PS[:, p * 1024 + eh * 512: p * 1024 + eh * 512 + 512],
                                    h_sl[c % 2][:, zi, tt * 128:(tt + 1) * 128],
                                    wbuf[:, WSL[c % 2][zi], eh * 512:(eh + 1) * 512],
                                    start=(zi == 0), stop=(zi == 3))
                w = []
                if zi == 0 and eh == 0:
                    w = [("dWB%d" % (c % 2), 16 * 4 * (c // 2 + 1)),
                         ("cA", m["gelu", c, (tt // 4) * 4 + 3])]
                    if g_ >= 3:
                        w.append(("cV", m["cpe", (g_ - 3) // 16, (g_ - 3) % 16]))
                    if g_ == 0:
                        w += [m["mde", 30], m["mde", 31], m["lnTe2", NT - 1]]
                last = (zi == 3 and eh == 1)
                r = sch.op("tensor", cm, waits=w, inc="cP" if last else None)
                if last:
                    m["cpz", c, tt] = r
        def ev(e, tt=tt, p=p):
            return e.tensor_tensor(out=x_t[:, tt, :], in0=PS[:, p * 1024:p * 1024 + 1024],
                                   in1=x_t[:, tt, :], op=ALU.add)
        m["cpe", c, tt] = sch.op("vector", ev, waits=[("cP", m["cpz", c, tt])], inc="cV")
        if c == NCK - 1:
            def yd(e, tt=tt):
                return e.dma_start(out=y_h[tt * 128:(tt + 1) * 128, :], in_=x_t[:, tt, :])
            sch.op("sync", yd, waits=[("cV", m["cpe", c, tt])], inc="dY", n=16)

    for q in range(NT):
        if 2 * q + 4 < 32:
            emit_md(2 * q + 4)
            emit_md(2 * q + 5)
        if q == 10:
            emit_wdma(0, (2, 3))
        if q >= 4:
            emit_xl2(q)
        po = (4 + q % 2) * 1024
        for c in range(NCH):
            def gt(e, q=q, c=c, po=po):
                return e.transpose(PSb[:, po + c * 128: po + c * 128 + 128],
                                   xlroll[:, q % 4, c * 128:c * 128 + 128], eye[:])
            w = W(m["xl2", q] if c == 0 else None,
                  ("cV", m["x2", NT - 1]) if q == 0 and c == 0 else None,
                  m["pte", NT - 1, NT - 1] if q == 1 and c == 0 else None,
                  m.get(("lnTe2", q - 2)) if q >= 2 and c == 0 else None)
            r = sch.op("tensor", gt, waits=w, inc="cP" if c == NCH - 1 else None)
            if c == NCH - 1:
                m["lnT2", q] = r
        def ge(e, q=q, po=po):
            src = PSb[:, po:po + 1024].rearrange("p (c k) -> p c k", k=128)
            return e.tensor_copy(out=xl_fm[:, :, q * 128:q * 128 + 128], in_=src)
        m["lnTe2", q] = ("cV", sch.op("vector", ge, waits=[("cP", m["lnT2", q])], inc="cV"))
        if q % 4 == 3:
            for j in range((q - 3), q + 1):
                emit_rails_one(0, j)
    x2l_fm = xl_fm

    for c in range(1, NCK):
        emit_wdma(c)
        for tt in range(16):
            emit_cproj_one(c - 1, tt)
            emit_rails_one(c, tt)
    for tt in range(16):
        emit_cproj_one(NCK - 1, tt)

    def fin(e):
        return e.nop()
    sch.op("sync", fin, waits=[("dY", 16 * 16)])

    with es, nc.Block() as blk:
        decos = {"sync": blk.sync, "tensor": blk.tensor, "vector": blk.vector,
                 "scalar": blk.scalar, "gpsimd": blk.gpsimd}
        for name in ENGS:
            def make(name):
                def emit(eng):
                    seen = {}
                    for waits, fn, inc, n in sch.ops[name]:
                        for s, v in waits:
                            if seen.get(s, 0) < v:
                                eng.wait_ge(sems[s], v)
                                seen[s] = v
                        inst = fn(eng)
                        if inc is not None:
                            inst.then_inc(sems[inc], n)
                return emit
            decos[name](make(name))
    return nc


def _host_prep(L0_attn_x, L0_attn_W, L0_ffn_rail_maker, L0_ffn_rail_up_proj, L0_ffn_w_cproj):
    x = np.ascontiguousarray(np.asarray(L0_attn_x))
    Wm = np.asarray(L0_attn_W)
    rm = np.asarray(L0_ffn_rail_maker)
    rup = np.asarray(L0_ffn_rail_up_proj)
    wc = np.asarray(L0_ffn_w_cproj)
    wkT = np.ascontiguousarray(Wm[:D].T)
    wvT = np.ascontiguousarray(Wm[D:].T)
    wcT = np.ascontiguousarray(wc.T)
    A = np.zeros((4, 8, 128, 128), np.float32)
    B = np.zeros((4, 8, 128, 512), np.float32)
    for h in range(4):
        for c in range(8):
            for j in range(8):
                r = 8 * c + j
                A[h, c, 16 * j:16 * j + 16, 16 * j:16 * j + 16] = rm[h, r].T
                B[h, c, 16 * j:16 * j + 16, 64 * j:64 * j + 64] = rup[h, r].T
    At = A.reshape(32, 128, 128)
    Ap = np.concatenate([np.concatenate([At[8 * t + k] for k in range(8)], axis=1)
                         for t in range(4)], axis=0)
    Bt = B.reshape(32, 128, 512)
    Bp = np.concatenate([np.concatenate([Bt[2 * t], Bt[2 * t + 1]], axis=1)
                         for t in range(16)], axis=0)
    abp = np.ascontiguousarray(np.concatenate([Ap, Bp], axis=0))
    return x, wkT, wvT, abp, wcT


def kernel(L0_attn_x, L0_attn_W, L0_ffn_rail_maker, L0_ffn_rail_up_proj, L0_ffn_w_cproj):
    x, wkT, wvT, abp, wcT = _host_prep(L0_attn_x, L0_attn_W, L0_ffn_rail_maker,
                                       L0_ffn_rail_up_proj, L0_ffn_w_cproj)
    nc = build_nc()
    in_maps = [{"x": np.ascontiguousarray(x[b]), "wkT": wkT, "wvT": wvT,
                "abp": abp, "wcT": wcT} for b in range(8)]
    res = run_bass_kernel_spmd(nc, in_maps, core_ids=list(range(8)))
    return np.stack([np.asarray(r["y"]) for r in res.results], axis=0)
